# version 04
# baseline (speedup 1.0000x reference)
import numpy as np
import ml_dtypes
import concourse.bass as bass
import concourse.mybir as mybir
from concourse.bass_utils import run_bass_kernel_spmd

F32 = mybir.dt.float32
BF16 = mybir.dt.bfloat16
FP16 = mybir.dt.float16
I32 = mybir.dt.int32
AF = mybir.ActivationFunctionType
ALU = mybir.AluOpType
AX = mybir.AxisListType

NCORES = 8
BPC = 4
T = 2048
E = 1024
HID = 3072
NHEAD = 16
D = 64
NEG = -1.0e9
NTILE = T // 128
FTILE = 256
NFT = (BPC * T) // FTILE

ENGS = ("sync", "gpsimd", "scalar", "vector", "tensor")


class Em:

    def __init__(self):
        self.ops = {e: [] for e in ENGS}
        self.cnt = {}

    def sem(self, *names):
        for n in names:
            self.cnt[n] = 0

    def op(self, eng, fn, wait=(), inc=None, inc_amt=1):
        w = tuple((s, v) for (s, v) in wait if v > 0)
        if inc is not None:
            self.cnt[inc] += inc_amt
        self.ops[eng].append((w, fn, inc, inc_amt))
        return self.cnt[inc] if inc is not None else None

    def now(self, sem):
        return self.cnt[sem]

    def replay(self, eng_name, engobj, semh):
        last = {}
        for (waits, fn, inc, amt) in self.ops[eng_name]:
            for (s, v) in waits:
                if last.get(s, 0) < v:
                    engobj.wait_ge(semh[s], v)
                    last[s] = v
            inst = fn(engobj) if fn is not None else None
            if inc is not None:
                assert inst is not None
                inst.then_inc(semh[inc], amt)


def build_program():
    nc = bass.Bass("TRN2", target_bir_lowering=False)

    x_d = nc.dram_tensor("x", [BPC, T, E], F32, kind="ExternalInput").ap()
    gum_d = nc.dram_tensor("gum", [BPC, T, T], FP16, kind="ExternalInput").ap()
    wqkT_d = nc.dram_tensor("wqkT", [E, 2 * E], F32, kind="ExternalInput").ap()
    bias_d = nc.dram_tensor("biasqk", [2 * E], F32, kind="ExternalInput").ap()
    wgT_d = nc.dram_tensor("wgT", [E, HID], I32, kind="ExternalInput").ap()
    wuT_d = nc.dram_tensor("wuT", [E, HID], I32, kind="ExternalInput").ap()
    wdT_d = nc.dram_tensor("wdT", [HID, E], I32, kind="ExternalInput").ap()
    ident_d = nc.dram_tensor("ident", [128, 128], BF16, kind="ExternalInput").ap()
    mask_d = nc.dram_tensor("maskneg", [128, 128], BF16, kind="ExternalInput").ap()
    ones_d = nc.dram_tensor("ones1", [1, 128], F32, kind="ExternalInput").ap()
    out_d = nc.dram_tensor("out", [BPC, T, E], F32, kind="ExternalOutput").ap()
    xn_scr = nc.dram_tensor("xn_scr", [BPC * T, E], BF16, kind="Internal").ap()
    g_scr = nc.dram_tensor("g_scr", [BPC * T, E], BF16, kind="Internal").ap()
    wqk_scr = nc.dram_tensor("wqk_scr", [E, 2 * E], BF16, kind="Internal").ap()
    xnT_scr = nc.dram_tensor("xnT_scr", [8, 128, BPC * T], BF16, kind="Internal").ap()
    gum0_d = nc.dram_tensor("gum0", [BPC, 128, 128], F32, kind="ExternalInput").ap()

    ARENA_BYTES = 211968
    arena_h = nc.alloc_sbuf_tensor("arena", [128, ARENA_BYTES], mybir.dt.uint8)
    ARBASE = nc.lookup_mls(arena_h).memorylocations[0].addr
    alloc = {}

    def sb(name, shape, dtype, off):
        h = nc.alloc_sbuf_tensor_at(name, shape, dtype, offset=ARBASE + off)
        alloc[name] = h
        return h.ap()

    GLOB0 = 0
    ident = sb("identsb", [128, 128], BF16, GLOB0 + 0)
    maskneg = sb("masksb", [128, 128], BF16, GLOB0 + 256)
    ones1 = sb("onessb", [1, 128], F32, GLOB0 + 512)
    biasS = sb("biasS", [128, 16], F32, GLOB0 + 1024)
    biasRaw = sb("biasRaw", [128, 16], F32, GLOB0 + 1088)
    bias1 = sb("bias1", [128, 1], F32, GLOB0 + 1152)
    smallA = sb("smallA", [128, 48], F32, GLOB0 + 1184)
    sc1 = sb("sc1", [1, 16], F32, GLOB0 + 1376)
    Zp = sb("Zp", [128, 4], F32, GLOB0 + 1440)
    cvecV = sb("cvecV", [128, 1], F32, GLOB0 + 1472)
    cvecP = sb("cvecP", [128, 1], F32, GLOB0 + 1504)
    z01 = sb("z01", [128, 1], F32, GLOB0 + 1536)
    Z2 = sb("Z2", [128, 2], F32, GLOB0 + 1568)
    rec2 = sb("rec2", [128, 1], F32, GLOB0 + 1600)
    AR = 2048

    wg_bf = sb("wg_bf", [128, 8, HID], BF16, AR + 0)
    wu_bf = sb("wu_bf", [128, 8, HID], BF16, AR + 49152)
    wd_bf = sb("wd_bf", [128, 24, E], BF16, AR + 98304)
    x_f = [sb(f"x_f{j}", [128, 2, E], F32, AR + 147456 + j * 8192) for j in range(2)]
    xn_f = [sb(f"xn_f{j}", [128, 2, E], BF16, AR + 163840 + j * 4096) for j in range(2)]
    xnT_f = sb("xnT_f", [128, 8, FTILE], BF16, AR + 172032)
    a_strip = sb("a_strip", [128, FTILE], BF16, AR + 176128)
    u_strip = sb("u_strip", [128, FTILE], BF16, AR + 176640)
    h_bf = sb("h_bf", [128, 24, FTILE], BF16, AR + 177152)
    g_f = [sb(f"g_f{j}", [128, 2, E], BF16, AR + 189440 + j * 4096) for j in range(2)]
    junk = sb("junk", [128, E], BF16, AR + 197632)
    stg = [sb(f"stg{j}", [128, 1024], I32, AR + 147456 + j * 4096) for j in range(3)]
    wstgF = [sb(f"wstgF{j}", [128, 1024], F32, AR + 159744 + j * 8192)
             for j in range(2)]
    wqk_stg = [sb(f"wqkstg{j}", [128, 1024], BF16, AR + 163840 + j * 2048)
               for j in range(2)]

    wqk_bf = sb("wqk_bf", [128, 8, 2 * E], BF16, AR + 0)
    QT = sb("QT", [128, 8, T], BF16, AR + 32768)
    KT_bd = sb("KT_bd", [128, 8, 2, T], BF16, AR + 65536)
    xn_a = sb("xn_a", [128, NTILE, E], BF16, AR + 131072)
    xnT3 = [sb(f"xnT3_{j}", [128, 8, 256], BF16, AR + 163840 + j * 4096)
            for j in range(3)]
    avgV = sb("avgV", [128, T], BF16, AR + 176128)
    avgP = sb("avgP", [128, T], BF16, AR + 180224)
    S0 = sb("S0", [128, T], BF16, AR + 184320)
    S1 = sb("S1", [128, T], BF16, AR + 188416)
    gum_b = [sb(f"gum_b{j}", [128, T], FP16, AR + 192512 + j * 4096) for j in range(2)]
    xres = sb("xres", [128, E], F32, AR + 200704)
    gt_b = [sb(f"gt_b{j}", [128, E], BF16, AR + 204800 + j * 2048) for j in range(2)]
    wT2 = sb("wT2", [128, 2, 128], BF16, AR + 208896)
    gum0_sb = sb("gum0_sb", [128, 128], F32, AR + 209408)

    ps_guards = []

    def ps(name, shape):
        g = nc.psum_tensor(name, shape, F32)
        h = g.__enter__()
        ps_guards.append(g)
        return h.ap()

    big0 = ps("big0", [128, 1024])
    big1 = ps("big1", [128, 1024])
    big2 = ps("big2", [128, 1024])
    small0 = ps("small0", [128, 512])
    small1 = ps("small1", [128, 512])
    psA = small0[:, 0:FTILE]
    psB = small1[:, 0:FTILE]
    psF = [big0, big1]
    ptF = [big2[:, 0:128], big2[:, 512:640]]
    region = [big0, big1]
    sOut = big2
    ptA = [small0[:, 0:128], small1[:, 0:128]]

    em = Em()
    SEMS = ("Lc", "Lx0", "Lx1", "Ls0", "Ls1", "Ls2", "Swq", "SxT", "Lg0", "Lg1", "Lg0f",
            "Lr0", "Lr1", "Lt0", "Lt1", "Lw0", "Lw1", "Lb",
            "Lq0", "Lq1", "Lq2", "Lq3", "Lx0a", "Lx1a", "Lx2a",
            "Lxt0", "Lxt1", "Lxt2",
            "Sxn0", "Sxn1", "Sg0", "Sg1", "So0", "So1", "P", "A", "V", "Pl")
    em.sem(*SEMS)

    em.op("sync", lambda e: e.dma_start(out=ident[:], in_=ident_d[:]), inc="Lc", inc_amt=16)
    em.op("sync", lambda e: e.dma_start(out=maskneg[:], in_=mask_d[:]), inc="Lc", inc_amt=16)
    em.op("sync", lambda e: e.dma_start(out=ones1[:], in_=ones_d[:]), inc="Lc", inc_amt=16)
    L_consts = em.now("Lc")

    QN = ("sync", "scalar", "gpsimd")
    cast_jobs = []
    for c in range(8):
        for h3 in range(3):
            cast_jobs.append((wgT_d[c * 128:(c + 1) * 128, h3 * 1024:(h3 + 1) * 1024],
                              wg_bf[:, c, h3 * 1024:(h3 + 1) * 1024]))
    for c in range(8):
        for h3 in range(3):
            cast_jobs.append((wuT_d[c * 128:(c + 1) * 128, h3 * 1024:(h3 + 1) * 1024],
                              wu_bf[:, c, h3 * 1024:(h3 + 1) * 1024]))
    for c in range(24):
        cast_jobs.append((wdT_d[c * 128:(c + 1) * 128, :], wd_bf[:, c, :]))
    stg_free = [("V", 0), ("V", 0), ("V", 0)]
    for j, (src, dst) in enumerate(cast_jobs):
        sl = j % 3
        em.op(QN[sl], (lambda e, s=src, t=stg[sl][:]: e.dma_start(out=t, in_=s)),
              wait=[stg_free[sl]], inc=f"Ls{sl}", inc_amt=16)
        lv = em.now(f"Ls{sl}")
        em.op("vector", (lambda e, s=stg[sl][:], t=dst: e.tensor_copy(out=t, in_=s)),
              wait=[(f"Ls{sl}", lv)], inc="V")
        stg_free[sl] = ("V", em.now("V"))
    wstg_free = [("V", 0), ("V", 0)]
    wqs_free = [("Swq", 0), ("Swq", 0)]
    for j in range(16):
        c, h = j // 2, j % 2
        sl = j % 2
        qn = QN[sl]
        em.op(qn, (lambda e, c=c, h=h, sl=sl: e.dma_start(
            out=wstgF[sl][:],
            in_=wqkT_d[c * 128:(c + 1) * 128, h * 1024:(h + 1) * 1024])),
            wait=[wstg_free[sl]], inc=f"Lw{sl}", inc_amt=16)
        lv = em.now(f"Lw{sl}")
        em.op("vector", (lambda e, sl=sl: e.tensor_copy(
            out=wqk_stg[sl][:], in_=wstgF[sl][:])),
            wait=[(f"Lw{sl}", lv), wqs_free[sl]], inc="V")
        wstg_free[sl] = ("V", em.now("V"))
        em.op(qn, (lambda e, c=c, h=h, sl=sl: e.dma_start(
            out=wqk_scr[c * 128:(c + 1) * 128, h * 1024:(h + 1) * 1024],
            in_=wqk_stg[sl][:])),
            wait=[("V", em.now("V"))], inc="Swq", inc_amt=16)
        wqs_free[sl] = ("Swq", em.now("Swq"))
    V_wcast = em.now("V")
    S_wq = em.now("Swq")

    xf_free = [("V", 0), ("V", 0)]
    gf_free = [("Sg0", 0), ("Sg1", 0)]
    xn_free = [("Sxn0", 0), ("Sxn1", 0)]
    psF_free = ("A", 0)
    psAB_free = [("A", 0), ("A", 0)]
    h_free = ("P", 0)
    au_free = ("V", 0)
    ptF_free = [("V", 0), ("V", 0)]
    NSUB = FTILE // 128

    for ft in range(NFT):
        b = ft // (T // FTILE)
        t0 = (ft % (T // FTILE)) * FTILE
        xb = ft % 2
        w_extra = [("V", V_wcast), ("Swq", S_wq)] if ft < 2 else []
        em.op("sync", (lambda e, b=b, t0=t0, xb=xb:
                       e.dma_start(out=x_f[xb][:],
                                   in_=x_d[b, t0:t0 + FTILE, :].rearrange("(s p) e -> p s e", p=128))),
              wait=[xf_free[xb]] + w_extra, inc=f"Lx{xb}", inc_amt=16)
        L_x = em.now(f"Lx{xb}")

        def vop(fn, wait=()):
            nonlocal vprev
            em.op("vector", fn, wait=list(wait) + [("V", vprev)], inc="V")
            vprev = em.now("V")
        vprev = em.now("V")
        wv = [(f"Lx{xb}", L_x)] + ([("V", V_wcast)] if ft < 2 else []) + [xn_free[xb]]
        m_v = smallA[:, 0:2]
        q2_v = smallA[:, 2:4]
        var_v = smallA[:, 4:6]
        r_v = smallA[:, 6:8]
        tmp_v = smallA[:, 8:10]
        tmp2_v = smallA[:, 10:12]
        for s in range(NSUB):
            vop((lambda e, s=s, xb=xb: e.reduce_sum(
                out=m_v[:, s:s + 1], in_=x_f[xb][:, s, :], axis=AX.X)), wait=wv)
            wv = []
            vop((lambda e, s=s, xb=xb: e.scalar_tensor_tensor(
                out=junk[:], in0=x_f[xb][:, s, :], scalar=1.0, in1=x_f[xb][:, s, :],
                op0=ALU.mult, op1=ALU.mult, accum_out=q2_v[:, s:s + 1])))
        vop(lambda e: e.tensor_scalar(
            out=m_v[:], in0=m_v[:], scalar1=1.0 / E, scalar2=None, op0=ALU.mult))
        vop(lambda e: e.tensor_tensor(
            out=tmp_v[:], in0=m_v[:], in1=m_v[:], op=ALU.mult))
        vop(lambda e: e.tensor_scalar(
            out=var_v[:], in0=q2_v[:], scalar1=1.0 / E, scalar2=None, op0=ALU.mult))
        vop(lambda e: e.tensor_tensor(
            out=var_v[:], in0=var_v[:], in1=tmp_v[:], op=ALU.subtract))
        vop(lambda e: e.tensor_scalar(
            out=var_v[:], in0=var_v[:], scalar1=1e-5, scalar2=None, op0=ALU.add))
        vop(lambda e: e.memset(r_v[:], 1.0))
        for _ in range(5):
            vop(lambda e: e.tensor_tensor(
                out=tmp_v[:], in0=var_v[:], in1=r_v[:], op=ALU.mult))
            vop(lambda e: e.tensor_tensor(
                out=tmp2_v[:], in0=tmp_v[:], in1=r_v[:], op=ALU.mult))
            vop(lambda e: e.tensor_scalar(
                out=tmp2_v[:], in0=tmp2_v[:], scalar1=-0.5, scalar2=1.5,
                op0=ALU.mult, op1=ALU.add))
            vop(lambda e: e.tensor_tensor(
                out=r_v[:], in0=r_v[:], in1=tmp2_v[:], op=ALU.mult))
        for s in range(NSUB):
            vop((lambda e, s=s, xb=xb: e.tensor_scalar(
                out=xn_f[xb][:, s, :], in0=x_f[xb][:, s, :],
                scalar1=m_v[:, s:s + 1], scalar2=r_v[:, s:s + 1],
                op0=ALU.subtract, op1=ALU.mult)))
        V_xn = em.now("V")
        xf_free[xb] = ("V", V_xn)

        em.op("gpsimd", (lambda e, b=b, t0=t0, xb=xb: e.dma_start(
            out=xn_scr[b * T + t0:b * T + t0 + FTILE, :].rearrange("(s p) e -> p s e", p=128),
            in_=xn_f[xb][:])), wait=[("V", V_xn)], inc=f"Sxn{xb}", inc_amt=16)
        xn_free[xb] = (f"Sxn{xb}", em.now(f"Sxn{xb}"))

        for s in range(NSUB):
            for c in range(8):
                k = s * 8 + c
                sl = k % 2
                em.op("tensor", (lambda e, s=s, c=c, sl=sl, xb=xb: e.matmul(
                    ptF[sl][:], xn_f[xb][:, s, c * 128:(c + 1) * 128], ident[:],
                    start=True, stop=True)),
                    wait=[("V", V_xn), ptF_free[sl], ("Lc", 48),
                          ("SxT", 16 * ft)], inc="P")
                pv = em.now("P")
                em.op("vector", (lambda e, s=s, c=c, sl=sl: e.tensor_copy(
                    out=xnT_f[:, c, s * 128:(s + 1) * 128], in_=ptF[sl][:])),
                    wait=[("P", pv)], inc="V")
                ptF_free[sl] = ("V", em.now("V"))
        V_xnT = em.now("V")

        em.op("scalar", (lambda e, b=b, t0=t0: e.dma_start(
            out=xnT_scr[:, :, b * T + t0:b * T + t0 + FTILE].rearrange(
                "ec p t -> p ec t"),
            in_=xnT_f[:])), wait=[("V", V_xnT)], inc="SxT", inc_amt=16)

        for m in range(24):
            for ec in range(8):
                em.op("tensor", (lambda e, m=m, ec=ec: e.matmul(
                    psA[:], wg_bf[:, ec, m * 128:(m + 1) * 128], xnT_f[:, ec, :],
                    start=(ec == 0), stop=(ec == 7))),
                    wait=([("V", V_xnT), psAB_free[0]] if ec == 0 else []),
                    inc="P" if ec == 7 else None)
            P_gate = em.now("P")
            for ec in range(8):
                em.op("tensor", (lambda e, m=m, ec=ec: e.matmul(
                    psB[:], wu_bf[:, ec, m * 128:(m + 1) * 128], xnT_f[:, ec, :],
                    start=(ec == 0), stop=(ec == 7))),
                    wait=([psAB_free[1]] if ec == 0 else []),
                    inc="P" if ec == 7 else None)
            P_up = em.now("P")
            em.op("scalar", (lambda e: e.activation(
                out=a_strip[:], in_=psA[:], func=AF.Silu, scale=1.0 / 127.0)),
                wait=[("P", P_gate), au_free], inc="A")
            psAB_free[0] = ("A", em.now("A"))
            em.op("scalar", (lambda e: e.activation(
                out=u_strip[:], in_=psB[:], func=AF.Copy)),
                wait=[("P", P_up)], inc="A")
            psAB_free[1] = ("A", em.now("A"))
            A_u = em.now("A")
            em.op("vector", (lambda e, m=m: e.tensor_tensor(
                out=h_bf[:, m, :], in0=a_strip[:], in1=u_strip[:], op=ALU.mult)),
                wait=[("A", A_u), (h_free if m == 0 else ("P", 0))], inc="V")
            au_free = ("V", em.now("V"))
        V_h = em.now("V")

        for m in range(24):
            vm = V_h - 23 + m
            for s2 in range(NSUB):
                for n2 in range(2):
                    em.op("tensor", (lambda e, m=m, s2=s2, n2=n2: e.matmul(
                        psF[s2][:, n2 * 512:(n2 + 1) * 512],
                        h_bf[:, m, s2 * 128:(s2 + 1) * 128],
                        wd_bf[:, m, n2 * 512:(n2 + 1) * 512],
                        start=(m == 0), stop=(m == 23))),
                        wait=([("V", vm)] + ([psF_free] if m == 0 and s2 == 0 and n2 == 0 else [])),
                        inc="P" if (m == 23 and s2 == NSUB - 1 and n2 == 1) else None)
        P_down = em.now("P")
        h_free = ("P", P_down)

        gb = ft % 2
        for s2 in range(NSUB):
            em.op("scalar", (lambda e, s2=s2, gb=gb: e.activation(
                out=g_f[gb][:, s2, :], in_=psF[s2][:], func=AF.Silu,
                scale=1.0 / (127.0 * 127.0))),
                wait=[("P", P_down)] + ([gf_free[gb]] if s2 == 0 else []),
                inc="A" if s2 == NSUB - 1 else None)
        A_g = em.now("A")
        psF_free = ("A", A_g)
        em.op("gpsimd", (lambda e, b=b, t0=t0, gb=gb: e.dma_start(
            out=g_scr[b * T + t0:b * T + t0 + FTILE, :].rearrange("(s p) e -> p s e", p=128),
            in_=g_f[gb][:])), wait=[("A", A_g)], inc=f"Sg{gb}", inc_amt=16)
        gf_free[gb] = (f"Sg{gb}", em.now(f"Sg{gb}"))

    Fbar = [("P", em.now("P")), ("A", em.now("A")), ("V", em.now("V")),
            ("Sxn0", em.now("Sxn0")), ("Sxn1", em.now("Sxn1")),
            ("Sg0", em.now("Sg0")), ("Sg1", em.now("Sg1")),
            ("SxT", em.now("SxT"))]

    AQ = ("sync", "scalar", "gpsimd")
    wqk_waits = []
    for c2 in range(4):
        qn = AQ[c2 % 3]
        em.op(qn, (lambda e, c2=c2: e.dma_start(
            out=wqk_bf[:, 2 * c2:2 * c2 + 2, :],
            in_=wqk_scr[c2 * 256:(c2 + 1) * 256, :].rearrange(
                "(c p) n -> p c n", p=128))),
            wait=Fbar + [("Swq", S_wq)], inc=f"Lq{c2}", inc_amt=16)
        wqk_waits.append((f"Lq{c2}", em.now(f"Lq{c2}")))
    for c in range(16):
        em.op("sync", (lambda e, c=c: e.dma_start(
            out=biasRaw[:, c:c + 1],
            in_=bias_d[c * 128:(c + 1) * 128].rearrange("(p o) -> p o", o=1))),
            inc="Lb", inc_amt=16)
    L_bias = em.now("Lb")
    em.op("vector", lambda e: e.tensor_scalar(
        out=biasS[:, 0:8], in0=biasRaw[:, 0:8], scalar1=0.125, scalar2=None, op0=ALU.mult),
        wait=[("Lb", L_bias)] + Fbar)
    em.op("vector", lambda e: e.tensor_copy(out=biasS[:, 8:16], in_=biasRaw[:, 8:16]),
          inc="V")
    V_bias = em.now("V")

    xn_a_lastuse = ("P", 0)
    quad_free = [[] for _ in range(12)]

    def quad_ap(qi):
        t_ = [big0, big1, sOut][qi // 4]
        c0 = (qi % 4) * 256
        return t_[:, c0:c0 + 256]

    pt_free = [("V", 0), ("V", 0)]
    regions_free = [("A", 0), ("A", 0)]
    E0_free = []
    E1_free = []
    sOut_free = ("V", 0)
    gum_free = [("V", 0), ("V", 0)]
    xres_free = ("So0", 0)
    gt_free = [("V", 0), ("V", 0)]
    avgV_free = ("V", 0)
    avgP_free = ("V", 0)
    z2_free = [("V", 0), ("V", 0)]
    xnT_slot_free = [("P", 0), ("P", 0), ("P", 0)]
    gum0_free = ("Pl", 0)
    last_stt = 0
    last_comb = 0
    lastT_P = ("P", 0)

    em.op("vector", lambda e: e.memset(KT_bd.rearrange("p c r t -> p (c r t)"), 0.0),
          wait=Fbar, inc="V")
    V_kbdz = em.now("V")

    for b in range(BPC):
        xn_tile_w = []
        for tt in range(NTILE):
            qn = AQ[tt % 3]
            em.op(qn, (lambda e, b=b, tt=tt: e.dma_start(
                out=xn_a[:, tt, :],
                in_=xn_scr[b * T + tt * 128:b * T + (tt + 1) * 128, :])),
                wait=[xn_a_lastuse] + (Fbar if b == 0 else []),
                inc=f"Lx{tt % 3}a", inc_amt=16)
            xn_tile_w.append((f"Lx{tt % 3}a", em.now(f"Lx{tt % 3}a")))

        if b == 0:
            for qi in range(12):
                quad_free[qi] = []
        else:
            for qi in range(4):
                quad_free[qi] = [regions_free[0]]
            for qi in range(4, 8):
                quad_free[qi] = [regions_free[1]]
            for qi in range(8, 12):
                quad_free[qi] = [sOut_free]
        for tb in range(4):
            lvh = [0, 0]
            sloth = [0, 0]
            for half in range(2):
                th = tb * 2 + half
                slot = th % 3
                sloth[half] = slot
                qn = ("sync", "scalar")[th % 2]
                em.op(qn, (lambda e, b=b, tb=tb, half=half, slot=slot: e.dma_start(
                    out=xnT3[slot][:],
                    in_=xnT_scr[:, :, b * T + tb * 512 + half * 256:
                                b * T + tb * 512 + half * 256 + 256].rearrange(
                                    "ec p t -> p ec t"))),
                    wait=[xnT_slot_free[slot]] + (Fbar if b == 0 and th < 3 else []),
                    inc=f"Lxt{slot}", inc_amt=16)
                lvh[half] = em.now(f"Lxt{slot}")
            for co in range(16):
                for half in range(2):
                    slot = sloth[half]
                    qi = (2 * co + half) % 12
                    for ec in range(8):
                        em.op("tensor", (lambda e, co=co, ec=ec, slot=slot, qi=qi:
                                         e.matmul(
                            quad_ap(qi), wqk_bf[:, ec, co * 128:(co + 1) * 128],
                            xnT3[slot][:, ec, :],
                            start=(ec == 0), stop=(ec == 7))),
                            wait=(([(f"Lxt{slot}", lvh[half])] + quad_free[qi]
                                   + (wqk_waits if b == 0 else []))
                                  if ec == 0 else []),
                            inc="P" if ec == 7 else None)
                    if co == 15:
                        xnT_slot_free[slot] = ("P", em.now("P"))
                P_co = em.now("P")
                qi0 = (2 * co) % 12
                t_ = [big0, big1, sOut][qi0 // 4]
                c0 = (qi0 % 4) * 256
                if co < 8:
                    em.op("scalar", (lambda e, co=co, t_=t_, c0=c0, tb=tb: e.activation(
                        out=QT[:, co, tb * 512:(tb + 1) * 512], in_=t_[:, c0:c0 + 512],
                        func=AF.Identity, bias=biasS[:, co:co + 1], scale=0.125)),
                        wait=[("P", P_co), ("V", V_bias)], inc="A")
                else:
                    cc = co - 8
                    em.op("scalar", (lambda e, cc=cc, co=co, t_=t_, c0=c0, tb=tb:
                                     e.activation(
                        out=KT_bd[0:64, cc, 0, tb * 512:(tb + 1) * 512],
                        in_=t_[0:64, c0:c0 + 512],
                        func=AF.Identity, bias=biasS[0:64, co:co + 1], scale=1.0)),
                        wait=[("P", P_co), ("V", V_bias)]
                        + ([("V", V_kbdz)] if b == 0 else []), inc="A")
                    em.op("scalar", (lambda e, cc=cc, co=co, t_=t_, c0=c0, tb=tb:
                                     e.activation(
                        out=KT_bd[64:128, cc, 1, tb * 512:(tb + 1) * 512],
                        in_=t_[64:128, c0:c0 + 512],
                        func=AF.Identity, bias=biasS[64:128, co:co + 1], scale=1.0)),
                        inc="A")
                qf = [("A", em.now("A"))]
                quad_free[qi0] = qf
                quad_free[(qi0 + 1) % 12] = qf
        A_proj = em.now("A")

        qm = smallA[:, 16:17]
        km = smallA[:, 17:18]
        qm_bf = smallA[:, 18:19].bitcast(BF16)[:, 0:1]
        km_bf = smallA[:, 20:21].bitcast(BF16)[:, 0:1]
        def nvop(fn, wait=()):
            nonlocal nbprev
            em.op("vector", fn, wait=list(wait) + [("V", nbprev)], inc="V")
            nbprev = em.now("V")
        nbprev = em.now("V")
        nvop(lambda e: e.reduce_max(
            out=qm[:], in_=QT.rearrange("p c t -> p (c t)"), axis=AX.X,
            apply_absolute_value=True), wait=[("A", A_proj)])
        nvop(lambda e: e.reduce_max(
            out=km[:], in_=KT_bd.rearrange("p c r t -> p (c r t)"), axis=AX.X,
            apply_absolute_value=True))
        nvop(lambda e: e.tensor_copy(out=qm_bf[:], in_=qm[:]))
        nvop(lambda e: e.tensor_copy(out=km_bf[:], in_=km[:]))
        V_qk = em.now("V")
        em.op("tensor", lambda e: e.matmul(ptA[0][0:1, :], qm_bf[:], ident[:],
                                           start=True, stop=True),
              wait=[("V", V_qk), pt_free[0]], inc="P")
        P_q = em.now("P")
        em.op("tensor", lambda e: e.matmul(ptA[1][0:1, :], km_bf[:], ident[:],
                                           start=True, stop=True),
              wait=[pt_free[1]], inc="P")
        P_k = em.now("P")
        nvop(lambda e: e.reduce_max(out=sc1[:, 0:1], in_=ptA[0][0:1, :], axis=AX.X),
             wait=[("P", P_q)])
        nvop(lambda e: e.reduce_max(out=sc1[:, 1:2], in_=ptA[1][0:1, :], axis=AX.X),
             wait=[("P", P_k)])
        nvop(lambda e: e.tensor_tensor(
            out=sc1[:, 2:3], in0=sc1[:, 0:1], in1=sc1[:, 1:2], op=ALU.mult))
        nvop(lambda e: e.tensor_scalar(
            out=sc1[:, 2:3], in0=sc1[:, 2:3], scalar1=64.0, scalar2=80.0,
            op0=ALU.mult, op1=ALU.min))
        nvop(lambda e: e.tensor_scalar(
            out=sc1[:, 2:3], in0=sc1[:, 2:3], scalar1=-1.0, scalar2=None, op0=ALU.mult))
        V_nb = em.now("V")
        pt_free[0] = ("V", V_nb)
        pt_free[1] = ("V", V_nb)
        em.op("tensor", lambda e: e.matmul(ptA[0][:, 0:1], ones1[:], sc1[:, 2:3],
                                           start=True, stop=True),
              wait=[("V", V_nb)], inc="P")
        P_b = em.now("P")
        em.op("vector", lambda e: e.tensor_copy(out=bias1[:], in_=ptA[0][:, 0:1]),
              wait=[("P", P_b)], inc="V")
        pt_free[0] = ("V", em.now("V"))
        V_bias1 = em.now("V")

        for i in range(NTILE):
            Lk = 128 * (i + 1)
            ib = i % 2
            Lh = Lk // 2
            gw = [gum_free[ib]] + (Fbar if b == 0 and i < 2 else [])
            em.op("sync", (lambda e, b=b, i=i, Lh=Lh, ib=ib: e.dma_start(
                out=gum_b[ib][:, 0:Lh], in_=gum_d[b, i * 128:(i + 1) * 128, 0:Lh])),
                wait=gw, inc=f"Lg{ib}", inc_amt=16)
            em.op("scalar", (lambda e, b=b, i=i, Lh=Lh, Lk=Lk, ib=ib: e.dma_start(
                out=gum_b[ib][:, Lh:Lk], in_=gum_d[b, i * 128:(i + 1) * 128, Lh:Lk])),
                wait=gw, inc=f"Lg{ib}", inc_amt=16)
            L_gum = em.now(f"Lg{ib}")
            if i == 0:
                em.op("sync", (lambda e, b=b: e.dma_start(
                    out=gum0_sb[:], in_=gum0_d[b])),
                    wait=[gum0_free] + (Fbar if b == 0 else []),
                    inc="Lg0f", inc_amt=16)
                L_gum0 = em.now("Lg0f")
            em.op("sync", (lambda e, b=b, i=i: e.dma_start(
                out=xres[:], in_=x_d[b, i * 128:(i + 1) * 128, :])),
                wait=[xres_free], inc="Lr0", inc_amt=16)
            L_xres = em.now("Lr0")
            em.op("sync", (lambda e, b=b, i=i, ib=ib: e.dma_start(
                out=gt_b[ib][:], in_=g_scr[b * T + i * 128:b * T + (i + 1) * 128, :])),
                wait=[gt_free[ib]], inc=f"Lt{ib}", inc_amt=16)
            L_gt = em.now(f"Lt{ib}")

            n_d = (Lk - 128) // 512
            accV_done = [0, 0]
            for h in range(NHEAD):
                hp = h % 2
                R = region[hp]
                S_ = S0 if hp == 0 else S1
                p = h // 2
                par = h % 2
                parts = [(0, min(Lk, 1024))] + ([(1024, Lk)] if Lk > 1024 else [])
                for (pi, (c0, c1)) in enumerate(parts):
                    nlo = c0 // 512
                    nhi = (c1 + 511) // 512
                    for n in range(nlo, nhi):
                        k0 = 512 * n
                        w_n = min(512, Lk - k0)
                        rs = k0 - c0
                        is_diag = (n == n_d)
                        last = (n == nhi - 1)
                        wv = []
                        if n == nlo:
                            wv.append(regions_free[hp])
                            if h == 0 and pi == 0:
                                wv.append(("A", A_proj))
                        em.op("tensor", (lambda e, p=p, par=par, i=i, k0=k0, w_n=w_n,
                                         rs=rs, R=R, is_diag=is_diag: e.matmul(
                            R[:, rs:rs + w_n],
                            QT[:, p, i * 128:(i + 1) * 128],
                            KT_bd[:, p, par, k0:k0 + w_n],
                            start=True, stop=not is_diag)),
                            wait=wv,
                            inc="P" if (last and not is_diag) else None)
                        if is_diag:
                            em.op("tensor", (lambda e, R=R, rs=rs, w_n=w_n: e.matmul(
                                R[:, rs + w_n - 128:rs + w_n],
                                ident[:], maskneg[:],
                                start=False, stop=True)),
                                inc="P" if last else None)
                    P_sc = em.now("P")
                    zc = hp * 2 + pi
                    ew = [("P", P_sc), ("V", V_bias1)]
                    if pi == 0:
                        ew += E0_free if hp == 0 else E1_free
                        ew.append(("V", accV_done[hp]))
                        if b == 0 and i == 0:
                            ew += Fbar
                    em.op("scalar", (lambda e, R=R, c0=c0, c1=c1, S_=S_, zc=zc:
                                     e.activation(
                        out=S_[:, c0:c1], in_=R[:, 0:c1 - c0], func=AF.Exp,
                        bias=bias1[:], scale=1.0,
                        accum_out=Zp[:, zc:zc + 1])),
                        wait=ew, inc="A")
                    regions_free[hp] = ("A", em.now("A"))
                A_exp = em.now("A")
                cw = [("A", A_exp), ("V", last_stt)]
                if len(parts) == 2:
                    em.op("vector", (lambda e, hp=hp: e.tensor_tensor(
                        out=z01[:], in0=Zp[:, 2 * hp:2 * hp + 1],
                        in1=Zp[:, 2 * hp + 1:2 * hp + 2], op=ALU.add)),
                        wait=cw, inc="V")
                    em.op("vector", lambda e: e.reciprocal(out=cvecV[:], in_=z01[:]),
                          wait=[("V", em.now("V"))], inc="V")
                elif i == 0:
                    em.op("vector", (lambda e, hp=hp: e.tensor_scalar(
                        out=z01[:], in0=Zp[:, 2 * hp:2 * hp + 1], scalar1=1e-37,
                        scalar2=None, op0=ALU.max)),
                        wait=cw, inc="V")
                    em.op("vector", lambda e: e.reciprocal(out=cvecV[:], in_=z01[:]),
                          wait=[("V", em.now("V"))], inc="V")
                else:
                    em.op("vector", (lambda e, hp=hp: e.reciprocal(
                        out=cvecV[:], in_=Zp[:, 2 * hp:2 * hp + 1])),
                        wait=cw, inc="V")
                V_c = em.now("V")
                if h == 0:
                    em.op("vector", (lambda e, S_=S_, Lk=Lk: e.tensor_scalar(
                        out=avgV[:, 0:Lk], in0=S_[:, 0:Lk], scalar1=cvecV[:],
                        scalar2=None, op0=ALU.mult)),
                        wait=[avgV_free, ("V", V_c)], inc="V")
                else:
                    em.op("vector", (lambda e, S_=S_, Lk=Lk: e.scalar_tensor_tensor(
                        out=avgV[:, 0:Lk], in0=S_[:, 0:Lk], scalar=cvecV[:],
                        in1=avgV[:, 0:Lk], op0=ALU.mult, op1=ALU.add)),
                        wait=[("V", V_c)], inc="V")
                last_stt = em.now("V")
                accV_done[hp] = last_stt

            em.op("gpsimd", (lambda e, Lk=Lk: e.tensor_scalar(
                out=S0[:, 0:Lk], in0=avgV[:, 0:Lk], scalar1=0.03125,
                scalar2=None, op0=ALU.mult)),
                wait=[("V", last_stt), ("A", A_exp)], inc="Pl")
            if i == 0:
                em.op("gpsimd", (lambda e, Lk=Lk: e.tensor_tensor(
                    out=S0[:, 0:Lk], in0=S0[:, 0:Lk],
                    in1=gum0_sb[:], op=ALU.add)),
                    wait=[("Lg0f", L_gum0), ("Pl", em.now("Pl"))], inc="Pl")
                gum0_free = ("Pl", em.now("Pl"))
            else:
                em.op("gpsimd", (lambda e, ib=ib, Lk=Lk: e.tensor_tensor(
                    out=S0[:, 0:Lk], in0=S0[:, 0:Lk],
                    in1=gum_b[ib][:, 0:Lk], op=ALU.add)),
                    wait=[(f"Lg{ib}", L_gum), ("Pl", em.now("Pl"))], inc="Pl")
            Pl_l2 = em.now("Pl")
            avgV_free = ("Pl", Pl_l2)
            gum_free[ib] = ("Pl", Pl_l2)
            em.op("scalar", (lambda e, ib=ib, Lk=Lk: e.activation(
                out=S1[:, 0:Lk], in_=S0[:, 0:Lk], func=AF.Exp,
                accum_out=Z2[:, ib:ib + 1])),
                wait=[("Pl", Pl_l2), ("V", last_stt), lastT_P, z2_free[ib]], inc="A")
            A_w = em.now("A")
            E0_free = [("A", A_w), ("Pl", Pl_l2)]
            if i == 0:
                em.op("vector", (lambda e, ib=ib: e.tensor_scalar(
                    out=z01[:], in0=Z2[:, ib:ib + 1], scalar1=1e-37, scalar2=None,
                    op0=ALU.max)),
                    wait=[("A", A_w), ("V", last_comb)], inc="V")
                em.op("vector", lambda e: e.reciprocal(out=rec2[:], in_=z01[:]),
                      wait=[("V", em.now("V"))], inc="V")
            else:
                em.op("vector", (lambda e, ib=ib: e.reciprocal(
                    out=rec2[:], in_=Z2[:, ib:ib + 1])),
                    wait=[("A", A_w), ("V", last_comb)], inc="V")
            V_rec2 = em.now("V")
            z2_free[ib] = ("V", V_rec2)

            for j in range(i + 1):
                sl2 = j % 2
                em.op("tensor", (lambda e, j=j, sl2=sl2: e.matmul(
                    ptA[sl2][:], S1[:, j * 128:(j + 1) * 128], ident[:],
                    start=True, stop=True)),
                    wait=[("A", A_w), pt_free[sl2]], inc="P")
                pv = em.now("P")
                em.op("vector", (lambda e, sl2=sl2: e.tensor_copy(
                    out=wT2[:, sl2, :], in_=ptA[sl2][:])),
                    wait=[("P", pv)], inc="V")
                ev = em.now("V")
                pt_free[sl2] = ("V", ev)
                for n2 in range(2):
                    em.op("tensor", (lambda e, j=j, sl2=sl2, n2=n2, i=i: e.matmul(
                        sOut[:, n2 * 512:(n2 + 1) * 512],
                        wT2[:, sl2, :],
                        xn_a[:, j, n2 * 512:(n2 + 1) * 512],
                        start=(j == 0), stop=(j == i))),
                        wait=([("V", ev), xn_tile_w[j]] +
                              ([sOut_free, ("A", A_proj)] if j == 0 and n2 == 0
                               else [])),
                        inc="P" if (j == i and n2 == 1) else None)
            P_s = em.now("P")
            lastT_P = ("P", P_s)
            E1_free = [("V", last_stt), ("P", P_s)]

            em.op("vector", (lambda e, ib=ib: e.scalar_tensor_tensor(
                out=gt_b[ib][:], in0=sOut[:], scalar=rec2[:], in1=gt_b[ib][:],
                op0=ALU.mult, op1=ALU.mult)),
                wait=[("P", P_s), (f"Lt{ib}", L_gt), ("V", V_rec2)], inc="V")
            if i == 0:
                em.op("vector", (lambda e, ib=ib: e.memset(gt_b[ib][0:1, :], 0.0)),
                      wait=[("V", em.now("V"))], inc="V")
            V_comb = em.now("V")
            last_comb = V_comb
            sOut_free = ("V", V_comb)
            em.op("gpsimd", (lambda e, ib=ib: e.tensor_tensor(
                out=xres[:], in0=xres[:], in1=gt_b[ib][:], op=ALU.add)),
                wait=[("Lr0", L_xres), ("V", V_comb)], inc="Pl")
            Pl_comb = em.now("Pl")
            gt_free[ib] = ("Pl", Pl_comb)
            em.op("gpsimd", (lambda e, b=b, i=i: e.dma_start(
                out=out_d[b, i * 128:(i + 1) * 128, :], in_=xres[:])),
                wait=[("Pl", Pl_comb)], inc="So0", inc_amt=16)
            xres_free = ("So0", em.now("So0"))

        xn_a_lastuse = ("P", em.now("P"))

    em.op("gpsimd", None, wait=[("So0", em.now("So0"))])

    sem_guards = []
    semh = {}
    for name in SEMS:
        g = nc.semaphore(f"sem_{name}")
        semh[name] = g.__enter__()
        sem_guards.append(g)
    blk = nc.Block()
    block = blk.__enter__()

    block.sync(lambda e: em.replay("sync", e, semh))
    block.gpsimd(lambda e: em.replay("gpsimd", e, semh))
    block.scalar(lambda e: em.replay("scalar", e, semh))
    block.vector(lambda e: em.replay("vector", e, semh))
    block.tensor(lambda e: em.replay("tensor", e, semh))

    blk.__exit__(None, None, None)
    for g in reversed(sem_guards):
        g.__exit__(None, None, None)
    for g in reversed(ps_guards):
        g.__exit__(None, None, None)
    return nc


_GUM = None
_GUM0 = None
_NC = None


def _gumbel_rows(flat_idx):
    lo = flat_idx.astype(np.uint32)
    k0 = np.uint32(0)
    k1 = np.uint32(42)
    ks = (k0, k1, np.uint32(k0 ^ k1 ^ np.uint32(0x1BD11BDA)))
    rot0 = (13, 15, 26, 6)
    rot1 = (17, 29, 16, 24)
    with np.errstate(over="ignore"):
        x0 = np.full_like(lo, ks[0])
        x1 = lo + ks[1]

        def rounds(x0, x1, rots):
            for r in rots:
                x0 = x0 + x1
                x1 = (x1 << np.uint32(r)) | (x1 >> np.uint32(32 - r))
                x1 = x1 ^ x0
            return x0, x1

        x0, x1 = rounds(x0, x1, rot0); x0 = x0 + ks[1]; x1 = x1 + ks[2] + np.uint32(1)
        x0, x1 = rounds(x0, x1, rot1); x0 = x0 + ks[2]; x1 = x1 + ks[0] + np.uint32(2)
        x0, x1 = rounds(x0, x1, rot0); x0 = x0 + ks[0]; x1 = x1 + ks[1] + np.uint32(3)
        x0, x1 = rounds(x0, x1, rot1); x0 = x0 + ks[1]; x1 = x1 + ks[2] + np.uint32(4)
        x0, x1 = rounds(x0, x1, rot0); x0 = x0 + ks[2]; x1 = x1 + ks[0] + np.uint32(5)
        bits = x0 ^ x1
    f = (bits >> np.uint32(9)) | np.uint32(0x3F800000)
    fl = f.view(np.float32) - np.float32(1.0)
    minval = np.float32(1e-6)
    maxval = np.float32(1.0 - 1e-6)
    u = np.maximum(minval, fl * (maxval - minval) + minval)
    return -np.log(-np.log(u))


def _gumbel():
    global _GUM, _GUM0
    if _GUM is None:
        g = np.empty((32, T, T), np.float16)
        g0 = np.empty((32, 128, 128), np.float32)
        base = np.arange(T * T, dtype=np.uint64)
        idx = np.arange(T)
        maskrow = idx[None, :] >= idx[:, None]
        for b in range(32):
            gb = (np.float32(0.5)
                  * _gumbel_rows(base + np.uint64(b) * np.uint64(T * T))
                  ).reshape(T, T)
            g0[b] = gb[0:128, 0:128]
            g0[b][maskrow[0:128, 0:128]] = np.float32(NEG)
            g[b] = gb.astype(np.float16)
        g[:, maskrow] = np.float16(-20000.0)
        _GUM = g
        _GUM0 = g0
    return _GUM


def kernel(L0_attn_hidden_states, L0_attn_in_proj_weight, L0_attn_in_proj_bias,
           L0_ffn_w_gate, L0_ffn_w_up, L0_ffn_w_down):
    global _NC
    x = np.ascontiguousarray(np.asarray(L0_attn_hidden_states, dtype=np.float32))
    wqkT = np.ascontiguousarray(np.asarray(L0_attn_in_proj_weight,
                                           dtype=np.float32)[:2 * E].T)
    biasqk = np.ascontiguousarray(np.asarray(L0_attn_in_proj_bias,
                                             dtype=np.float32)[:2 * E])
    wgT = np.ascontiguousarray(np.asarray(L0_ffn_w_gate, dtype=np.int32).T)
    wuT = np.ascontiguousarray(np.asarray(L0_ffn_w_up, dtype=np.int32).T)
    wdT = np.ascontiguousarray(np.asarray(L0_ffn_w_down, dtype=np.int32).T)
    gum = _gumbel()
    ident = np.eye(128, dtype=ml_dtypes.bfloat16)
    r = np.arange(128)
    maskneg = np.where(r[None, :] >= r[:, None], np.float32(NEG),
                       np.float32(0.0)).astype(ml_dtypes.bfloat16)
    ones1 = np.ones((1, 128), np.float32)
    if _NC is None:
        _NC = build_program()
    in_maps = []
    for c in range(NCORES):
        in_maps.append({
            "x": x[c * BPC:(c + 1) * BPC],
            "gum": gum[c * BPC:(c + 1) * BPC],
            "gum0": _GUM0[c * BPC:(c + 1) * BPC],
            "wqkT": wqkT, "biasqk": biasqk,
            "wgT": wgT, "wuT": wuT, "wdT": wdT,
            "ident": ident, "maskneg": maskneg, "ones1": ones1,
        })
    res = run_bass_kernel_spmd(_NC, in_maps, core_ids=list(range(NCORES)))
    return np.concatenate([r_["out"] for r_ in res.results], axis=0)


# version 06
# speedup vs baseline: 1.1963x; 1.1963x over previous
import numpy as np
import ml_dtypes
import concourse.bass as bass
import concourse.mybir as mybir
from concourse.bass_utils import run_bass_kernel_spmd

F32 = mybir.dt.float32
BF16 = mybir.dt.bfloat16
FP16 = mybir.dt.float16
I32 = mybir.dt.int32
AF = mybir.ActivationFunctionType
ALU = mybir.AluOpType
AX = mybir.AxisListType

NCORES = 8
BPC = 4
T = 2048
E = 1024
HID = 3072
NHEAD = 16
D = 64
NEG = -1.0e9
NTILE = T // 128
FTILE = 256
NFT = (BPC * T) // FTILE

ENGS = ("sync", "gpsimd", "scalar", "vector", "tensor")


class Em:

    def __init__(self):
        self.ops = {e: [] for e in ENGS}
        self.cnt = {}

    def sem(self, *names):
        for n in names:
            self.cnt[n] = 0

    def op(self, eng, fn, wait=(), inc=None, inc_amt=1):
        w = tuple((s, v) for (s, v) in wait if v > 0)
        if inc is not None:
            self.cnt[inc] += inc_amt
        self.ops[eng].append((w, fn, inc, inc_amt))
        return self.cnt[inc] if inc is not None else None

    def now(self, sem):
        return self.cnt[sem]

    def replay(self, eng_name, engobj, semh):
        last = {}
        for (waits, fn, inc, amt) in self.ops[eng_name]:
            for (s, v) in waits:
                if last.get(s, 0) < v:
                    engobj.wait_ge(semh[s], v)
                    last[s] = v
            inst = fn(engobj) if fn is not None else None
            if inc is not None:
                assert inst is not None
                inst.then_inc(semh[inc], amt)


def build_program():
    nc = bass.Bass("TRN2", target_bir_lowering=False)

    x_d = nc.dram_tensor("x", [BPC, T, E], F32, kind="ExternalInput").ap()
    gum_d = nc.dram_tensor("gum", [BPC, T, T], FP16, kind="ExternalInput").ap()
    wqkT_d = nc.dram_tensor("wqkT", [E, 2 * E], F32, kind="ExternalInput").ap()
    bias_d = nc.dram_tensor("biasqk", [2 * E], F32, kind="ExternalInput").ap()
    wgT_d = nc.dram_tensor("wgT", [E, HID], I32, kind="ExternalInput").ap()
    wuT_d = nc.dram_tensor("wuT", [E, HID], I32, kind="ExternalInput").ap()
    wdT_d = nc.dram_tensor("wdT", [HID, E], I32, kind="ExternalInput").ap()
    ident_d = nc.dram_tensor("ident", [128, 128], BF16, kind="ExternalInput").ap()
    mask_d = nc.dram_tensor("maskneg", [128, 128], BF16, kind="ExternalInput").ap()
    ones_d = nc.dram_tensor("ones1", [1, 128], F32, kind="ExternalInput").ap()
    out_d = nc.dram_tensor("out", [BPC, T, E], F32, kind="ExternalOutput").ap()
    xn_scr = nc.dram_tensor("xn_scr", [BPC * T, E], BF16, kind="Internal").ap()
    g_scr = nc.dram_tensor("g_scr", [BPC * T, E], BF16, kind="Internal").ap()
    wqk_scr = nc.dram_tensor("wqk_scr", [E, 2 * E], BF16, kind="Internal").ap()
    xnT_scr = nc.dram_tensor("xnT_scr", [8, 128, BPC * T], BF16, kind="Internal").ap()
    gum0_d = nc.dram_tensor("gum0", [BPC, 128, 128], F32, kind="ExternalInput").ap()

    ARENA_BYTES = 211968
    arena_h = nc.alloc_sbuf_tensor("arena", [128, ARENA_BYTES], mybir.dt.uint8)
    ARBASE = nc.lookup_mls(arena_h).memorylocations[0].addr
    alloc = {}

    def sb(name, shape, dtype, off):
        h = nc.alloc_sbuf_tensor_at(name, shape, dtype, offset=ARBASE + off)
        alloc[name] = h
        return h.ap()

    GLOB0 = 0
    ident = sb("identsb", [128, 128], BF16, GLOB0 + 0)
    maskneg = sb("masksb", [128, 128], BF16, GLOB0 + 256)
    ones1 = sb("onessb", [1, 128], F32, GLOB0 + 512)
    biasS = sb("biasS", [128, 16], F32, GLOB0 + 1024)
    biasRaw = sb("biasRaw", [128, 16], F32, GLOB0 + 1088)
    bias1 = sb("bias1", [128, 1], F32, GLOB0 + 1152)
    smallA = sb("smallA", [128, 48], F32, GLOB0 + 1184)
    sc1 = sb("sc1", [1, 16], F32, GLOB0 + 1376)
    Zp = sb("Zp", [128, 8], F32, GLOB0 + 1440)
    cvecV = sb("cvecV", [128, 1], F32, GLOB0 + 1472)
    cvecP = sb("cvecP", [128, 1], F32, GLOB0 + 1504)
    z01 = sb("z01", [128, 1], F32, GLOB0 + 1536)
    Z2 = sb("Z2", [128, 2], F32, GLOB0 + 1568)
    rec2 = sb("rec2", [128, 1], F32, GLOB0 + 1600)
    AR = 2048

    wg_bf = sb("wg_bf", [128, 8, HID], BF16, AR + 0)
    wu_bf = sb("wu_bf", [128, 8, HID], BF16, AR + 49152)
    wd_bf = sb("wd_bf", [128, 24, E], BF16, AR + 98304)
    x_f = [sb(f"x_f{j}", [128, 2, E], F32, AR + 147456 + j * 8192) for j in range(2)]
    xn_f = [sb(f"xn_f{j}", [128, 2, E], BF16, AR + 163840 + j * 4096) for j in range(2)]
    xnT_f = sb("xnT_f", [128, 8, FTILE], BF16, AR + 172032)
    a_strip = sb("a_strip", [128, FTILE], BF16, AR + 176128)
    u_strip = sb("u_strip", [128, FTILE], BF16, AR + 176640)
    h_bf = sb("h_bf", [128, 24, FTILE], BF16, AR + 177152)
    g_f = [sb(f"g_f{j}", [128, 2, E], BF16, AR + 189440 + j * 4096) for j in range(2)]
    junk = sb("junk", [128, E], BF16, AR + 197632)
    stg = [sb(f"stg{j}", [128, 1024], I32, AR + 147456 + j * 4096) for j in range(3)]
    wstgF = [sb(f"wstgF{j}", [128, 1024], F32, AR + 159744 + j * 8192)
             for j in range(2)]
    wqk_stg = [sb(f"wqkstg{j}", [128, 1024], BF16, AR + 163840 + j * 2048)
               for j in range(2)]

    wqk_bf = sb("wqk_bf", [128, 8, 2 * E], BF16, AR + 0)
    QT = sb("QT", [128, 8, T], BF16, AR + 32768)
    KT_bd = sb("KT_bd", [128, 8, 2, T], BF16, AR + 65536)
    xn_a = sb("xn_a", [128, NTILE, E], BF16, AR + 131072)
    xnT3 = [sb(f"xnT3_{j}", [128, 8, 256], BF16, AR + 163840 + j * 4096)
            for j in range(2)]
    avgV = sb("avgV", [128, T], BF16, AR + 172032)
    Sl = [sb(f"S{j}", [128, T], BF16, AR + 176128 + j * 4096)
          for j in range(4)]
    S0, S1 = Sl[0], Sl[1]
    gum_b = [sb(f"gum_b{j}", [128, T], FP16, AR + 192512 + j * 4096) for j in range(2)]
    xres = sb("xres", [128, E], F32, AR + 200704)
    gt_b = [sb(f"gt_b{j}", [128, E], BF16, AR + 204800 + j * 2048) for j in range(2)]
    wT2 = sb("wT2", [128, 2, 128], BF16, AR + 208896)
    gum0_sb = sb("gum0_sb", [128, 128], F32, AR + 209408)

    ps_guards = []

    def ps(name, shape):
        g = nc.psum_tensor(name, shape, F32)
        h = g.__enter__()
        ps_guards.append(g)
        return h.ap()

    big0 = ps("big0", [128, 1024])
    big1 = ps("big1", [128, 1024])
    big2 = ps("big2", [128, 1024])
    small0 = ps("small0", [128, 512])
    small1 = ps("small1", [128, 512])
    psA = small0[:, 0:FTILE]
    psB = small1[:, 0:FTILE]
    psF = [big0, big1]
    ptF = [big2[:, 0:128], big2[:, 512:640]]
    region = [big0, big1]
    sOut = big2
    ptA = [small0[:, 0:128], small1[:, 0:128]]

    em = Em()
    SEMS = ("Lc", "Lx0", "Lx1", "Ls0", "Ls1", "Ls2", "Swq", "SxT", "Lg0", "Lg1", "Lg0f",
            "Lr0", "Lr1", "Lt0", "Lt1", "Lw0", "Lw1", "Lb",
            "Lq0", "Lq1", "Lq2", "Lq3", "Lx0a", "Lx1a", "Lx2a",
            "Lxt0", "Lxt1", "Lxt2",
            "Sxn0", "Sxn1", "Sg0", "Sg1", "So0", "So1", "P", "A", "V", "Pl")
    em.sem(*SEMS)

    em.op("sync", lambda e: e.dma_start(out=ident[:], in_=ident_d[:]), inc="Lc", inc_amt=16)
    em.op("sync", lambda e: e.dma_start(out=maskneg[:], in_=mask_d[:]), inc="Lc", inc_amt=16)
    em.op("sync", lambda e: e.dma_start(out=ones1[:], in_=ones_d[:]), inc="Lc", inc_amt=16)
    L_consts = em.now("Lc")

    QN = ("sync", "scalar", "gpsimd")
    cast_jobs = []
    for c in range(8):
        for h3 in range(3):
            cast_jobs.append((wgT_d[c * 128:(c + 1) * 128, h3 * 1024:(h3 + 1) * 1024],
                              wg_bf[:, c, h3 * 1024:(h3 + 1) * 1024]))
    for c in range(8):
        for h3 in range(3):
            cast_jobs.append((wuT_d[c * 128:(c + 1) * 128, h3 * 1024:(h3 + 1) * 1024],
                              wu_bf[:, c, h3 * 1024:(h3 + 1) * 1024]))
    for c in range(24):
        cast_jobs.append((wdT_d[c * 128:(c + 1) * 128, :], wd_bf[:, c, :]))
    stg_free = [("V", 0), ("V", 0), ("V", 0)]
    for j, (src, dst) in enumerate(cast_jobs):
        sl = j % 3
        em.op(QN[sl], (lambda e, s=src, t=stg[sl][:]: e.dma_start(out=t, in_=s)),
              wait=[stg_free[sl]], inc=f"Ls{sl}", inc_amt=16)
        lv = em.now(f"Ls{sl}")
        em.op("vector", (lambda e, s=stg[sl][:], t=dst: e.tensor_copy(out=t, in_=s)),
              wait=[(f"Ls{sl}", lv)], inc="V")
        stg_free[sl] = ("V", em.now("V"))
    wstg_free = [("V", 0), ("V", 0)]
    wqs_free = [("Swq", 0), ("Swq", 0)]
    for j in range(16):
        c, h = j // 2, j % 2
        sl = j % 2
        qn = QN[sl]
        em.op(qn, (lambda e, c=c, h=h, sl=sl: e.dma_start(
            out=wstgF[sl][:],
            in_=wqkT_d[c * 128:(c + 1) * 128, h * 1024:(h + 1) * 1024])),
            wait=[wstg_free[sl]], inc=f"Lw{sl}", inc_amt=16)
        lv = em.now(f"Lw{sl}")
        em.op("vector", (lambda e, sl=sl: e.tensor_copy(
            out=wqk_stg[sl][:], in_=wstgF[sl][:])),
            wait=[(f"Lw{sl}", lv), wqs_free[sl]], inc="V")
        wstg_free[sl] = ("V", em.now("V"))
        em.op(qn, (lambda e, c=c, h=h, sl=sl: e.dma_start(
            out=wqk_scr[c * 128:(c + 1) * 128, h * 1024:(h + 1) * 1024],
            in_=wqk_stg[sl][:])),
            wait=[("V", em.now("V"))], inc="Swq", inc_amt=16)
        wqs_free[sl] = ("Swq", em.now("Swq"))
    V_wcast = em.now("V")
    S_wq = em.now("Swq")

    xf_free = [("V", 0), ("V", 0)]
    gf_free = [("Sg0", 0), ("Sg1", 0)]
    xn_free = [("Sxn0", 0), ("Sxn1", 0)]
    psF_free = ("A", 0)
    psAB_free = [("A", 0), ("A", 0)]
    h_free = ("P", 0)
    au_free = ("V", 0)
    ptF_free = [("V", 0), ("V", 0)]
    NSUB = FTILE // 128

    for ft in range(NFT):
        b = ft // (T // FTILE)
        t0 = (ft % (T // FTILE)) * FTILE
        xb = ft % 2
        w_extra = [("V", V_wcast), ("Swq", S_wq)] if ft < 2 else []
        em.op("sync", (lambda e, b=b, t0=t0, xb=xb:
                       e.dma_start(out=x_f[xb][:],
                                   in_=x_d[b, t0:t0 + FTILE, :].rearrange("(s p) e -> p s e", p=128))),
              wait=[xf_free[xb]] + w_extra, inc=f"Lx{xb}", inc_amt=16)
        L_x = em.now(f"Lx{xb}")

        def vop(fn, wait=()):
            nonlocal vprev
            em.op("vector", fn, wait=list(wait) + [("V", vprev)], inc="V")
            vprev = em.now("V")
        vprev = em.now("V")
        wv = [(f"Lx{xb}", L_x)] + ([("V", V_wcast)] if ft < 2 else []) + [xn_free[xb]]
        m_v = smallA[:, 0:2]
        q2_v = smallA[:, 2:4]
        var_v = smallA[:, 4:6]
        r_v = smallA[:, 6:8]
        tmp_v = smallA[:, 8:10]
        tmp2_v = smallA[:, 10:12]
        for s in range(NSUB):
            vop((lambda e, s=s, xb=xb: e.reduce_sum(
                out=m_v[:, s:s + 1], in_=x_f[xb][:, s, :], axis=AX.X)), wait=wv)
            wv = []
            vop((lambda e, s=s, xb=xb: e.scalar_tensor_tensor(
                out=junk[:], in0=x_f[xb][:, s, :], scalar=1.0, in1=x_f[xb][:, s, :],
                op0=ALU.mult, op1=ALU.mult, accum_out=q2_v[:, s:s + 1])))
        vop(lambda e: e.tensor_scalar(
            out=m_v[:], in0=m_v[:], scalar1=1.0 / E, scalar2=None, op0=ALU.mult))
        vop(lambda e: e.tensor_tensor(
            out=tmp_v[:], in0=m_v[:], in1=m_v[:], op=ALU.mult))
        vop(lambda e: e.tensor_scalar(
            out=var_v[:], in0=q2_v[:], scalar1=1.0 / E, scalar2=None, op0=ALU.mult))
        vop(lambda e: e.tensor_tensor(
            out=var_v[:], in0=var_v[:], in1=tmp_v[:], op=ALU.subtract))
        vop(lambda e: e.tensor_scalar(
            out=var_v[:], in0=var_v[:], scalar1=1e-5, scalar2=None, op0=ALU.add))
        vop(lambda e: e.memset(r_v[:], 1.0))
        for _ in range(5):
            vop(lambda e: e.tensor_tensor(
                out=tmp_v[:], in0=var_v[:], in1=r_v[:], op=ALU.mult))
            vop(lambda e: e.tensor_tensor(
                out=tmp2_v[:], in0=tmp_v[:], in1=r_v[:], op=ALU.mult))
            vop(lambda e: e.tensor_scalar(
                out=tmp2_v[:], in0=tmp2_v[:], scalar1=-0.5, scalar2=1.5,
                op0=ALU.mult, op1=ALU.add))
            vop(lambda e: e.tensor_tensor(
                out=r_v[:], in0=r_v[:], in1=tmp2_v[:], op=ALU.mult))
        for s in range(NSUB):
            vop((lambda e, s=s, xb=xb: e.tensor_scalar(
                out=xn_f[xb][:, s, :], in0=x_f[xb][:, s, :],
                scalar1=m_v[:, s:s + 1], scalar2=r_v[:, s:s + 1],
                op0=ALU.subtract, op1=ALU.mult)))
        V_xn = em.now("V")
        xf_free[xb] = ("V", V_xn)

        em.op("gpsimd", (lambda e, b=b, t0=t0, xb=xb: e.dma_start(
            out=xn_scr[b * T + t0:b * T + t0 + FTILE, :].rearrange("(s p) e -> p s e", p=128),
            in_=xn_f[xb][:])), wait=[("V", V_xn)], inc=f"Sxn{xb}", inc_amt=16)
        xn_free[xb] = (f"Sxn{xb}", em.now(f"Sxn{xb}"))

        for s in range(NSUB):
            for c in range(8):
                k = s * 8 + c
                sl = k % 2
                em.op("tensor", (lambda e, s=s, c=c, sl=sl, xb=xb: e.matmul(
                    ptF[sl][:], xn_f[xb][:, s, c * 128:(c + 1) * 128], ident[:],
                    start=True, stop=True)),
                    wait=[("V", V_xn), ptF_free[sl], ("Lc", 48),
                          ("SxT", 16 * ft)], inc="P")
                pv = em.now("P")
                em.op("vector", (lambda e, s=s, c=c, sl=sl: e.tensor_copy(
                    out=xnT_f[:, c, s * 128:(s + 1) * 128], in_=ptF[sl][:])),
                    wait=[("P", pv)], inc="V")
                ptF_free[sl] = ("V", em.now("V"))
        V_xnT = em.now("V")

        em.op("scalar", (lambda e, b=b, t0=t0: e.dma_start(
            out=xnT_scr[:, :, b * T + t0:b * T + t0 + FTILE].rearrange(
                "ec p t -> p ec t"),
            in_=xnT_f[:])), wait=[("V", V_xnT)], inc="SxT", inc_amt=16)

        for m in range(24):
            for ec in range(8):
                em.op("tensor", (lambda e, m=m, ec=ec: e.matmul(
                    psA[:], wg_bf[:, ec, m * 128:(m + 1) * 128], xnT_f[:, ec, :],
                    start=(ec == 0), stop=(ec == 7))),
                    wait=([("V", V_xnT), psAB_free[0]] if ec == 0 else []),
                    inc="P" if ec == 7 else None)
            P_gate = em.now("P")
            for ec in range(8):
                em.op("tensor", (lambda e, m=m, ec=ec: e.matmul(
                    psB[:], wu_bf[:, ec, m * 128:(m + 1) * 128], xnT_f[:, ec, :],
                    start=(ec == 0), stop=(ec == 7))),
                    wait=([psAB_free[1]] if ec == 0 else []),
                    inc="P" if ec == 7 else None)
            P_up = em.now("P")
            em.op("scalar", (lambda e: e.activation(
                out=a_strip[:], in_=psA[:], func=AF.Silu, scale=1.0 / 127.0)),
                wait=[("P", P_gate), au_free], inc="A")
            psAB_free[0] = ("A", em.now("A"))
            em.op("scalar", (lambda e: e.activation(
                out=u_strip[:], in_=psB[:], func=AF.Copy)),
                wait=[("P", P_up)], inc="A")
            psAB_free[1] = ("A", em.now("A"))
            A_u = em.now("A")
            em.op("vector", (lambda e, m=m: e.tensor_tensor(
                out=h_bf[:, m, :], in0=a_strip[:], in1=u_strip[:], op=ALU.mult)),
                wait=[("A", A_u), (h_free if m == 0 else ("P", 0))], inc="V")
            au_free = ("V", em.now("V"))
        V_h = em.now("V")

        for m in range(24):
            vm = V_h - 23 + m
            for s2 in range(NSUB):
                for n2 in range(2):
                    em.op("tensor", (lambda e, m=m, s2=s2, n2=n2: e.matmul(
                        psF[s2][:, n2 * 512:(n2 + 1) * 512],
                        h_bf[:, m, s2 * 128:(s2 + 1) * 128],
                        wd_bf[:, m, n2 * 512:(n2 + 1) * 512],
                        start=(m == 0), stop=(m == 23))),
                        wait=([("V", vm)] + ([psF_free] if m == 0 and s2 == 0 and n2 == 0 else [])),
                        inc="P" if (m == 23 and s2 == NSUB - 1 and n2 == 1) else None)
        P_down = em.now("P")
        h_free = ("P", P_down)

        gb = ft % 2
        for s2 in range(NSUB):
            em.op("scalar", (lambda e, s2=s2, gb=gb: e.activation(
                out=g_f[gb][:, s2, :], in_=psF[s2][:], func=AF.Silu,
                scale=1.0 / (127.0 * 127.0))),
                wait=[("P", P_down)] + ([gf_free[gb]] if s2 == 0 else []),
                inc="A" if s2 == NSUB - 1 else None)
        A_g = em.now("A")
        psF_free = ("A", A_g)
        em.op("gpsimd", (lambda e, b=b, t0=t0, gb=gb: e.dma_start(
            out=g_scr[b * T + t0:b * T + t0 + FTILE, :].rearrange("(s p) e -> p s e", p=128),
            in_=g_f[gb][:])), wait=[("A", A_g)], inc=f"Sg{gb}", inc_amt=16)
        gf_free[gb] = (f"Sg{gb}", em.now(f"Sg{gb}"))

    Fbar = [("P", em.now("P")), ("A", em.now("A")), ("V", em.now("V")),
            ("Sxn0", em.now("Sxn0")), ("Sxn1", em.now("Sxn1")),
            ("Sg0", em.now("Sg0")), ("Sg1", em.now("Sg1")),
            ("SxT", em.now("SxT"))]

    AQ = ("sync", "scalar", "gpsimd")
    wqk_waits = []
    for c2 in range(4):
        qn = AQ[c2 % 3]
        em.op(qn, (lambda e, c2=c2: e.dma_start(
            out=wqk_bf[:, 2 * c2:2 * c2 + 2, :],
            in_=wqk_scr[c2 * 256:(c2 + 1) * 256, :].rearrange(
                "(c p) n -> p c n", p=128))),
            wait=Fbar + [("Swq", S_wq)], inc=f"Lq{c2}", inc_amt=16)
        wqk_waits.append((f"Lq{c2}", em.now(f"Lq{c2}")))
    for c in range(16):
        em.op("sync", (lambda e, c=c: e.dma_start(
            out=biasRaw[:, c:c + 1],
            in_=bias_d[c * 128:(c + 1) * 128].rearrange("(p o) -> p o", o=1))),
            inc="Lb", inc_amt=16)
    L_bias = em.now("Lb")
    em.op("vector", lambda e: e.tensor_scalar(
        out=biasS[:, 0:8], in0=biasRaw[:, 0:8], scalar1=0.125, scalar2=None, op0=ALU.mult),
        wait=[("Lb", L_bias)] + Fbar)
    em.op("vector", lambda e: e.tensor_copy(out=biasS[:, 8:16], in_=biasRaw[:, 8:16]),
          inc="V")
    V_bias = em.now("V")

    xn_a_lastuse = ("P", 0)
    quad_free = [[] for _ in range(12)]

    def quad_ap(qi):
        t_ = [big0, big1, sOut][qi // 4]
        c0 = (qi % 4) * 256
        return t_[:, c0:c0 + 256]

    pt_free = [("V", 0), ("V", 0)]
    regions_free = [("A", 0), ("A", 0)]
    E0_free = []
    E1_free = []
    sOut_free = ("V", 0)
    gum_free = [("V", 0), ("V", 0)]
    xres_free = ("So0", 0)
    gt_free = [("V", 0), ("V", 0)]
    avgV_free = ("V", 0)
    avgP_free = ("V", 0)
    z2_free = [("V", 0), ("V", 0)]
    xnT_slot_free = [("P", 0), ("P", 0)]
    gum0_free = ("Pl", 0)
    accV_done = [0, 0, 0, 0]
    last_stt = 0
    last_comb = 0
    lastT_P = ("P", 0)

    em.op("vector", lambda e: e.memset(KT_bd.rearrange("p c r t -> p (c r t)"), 0.0),
          wait=Fbar, inc="V")
    V_kbdz = em.now("V")
    em.op("vector", lambda e: e.memset(bias1[:], -50.0), inc="V")
    V_bias1 = em.now("V")

    for b in range(BPC):
        xn_tile_w = []
        for tt in range(NTILE):
            qn = AQ[tt % 3]
            em.op(qn, (lambda e, b=b, tt=tt: e.dma_start(
                out=xn_a[:, tt, :],
                in_=xn_scr[b * T + tt * 128:b * T + (tt + 1) * 128, :])),
                wait=[xn_a_lastuse] + (Fbar if b == 0 else []),
                inc=f"Lx{tt % 3}a", inc_amt=16)
            xn_tile_w.append((f"Lx{tt % 3}a", em.now(f"Lx{tt % 3}a")))

        if b == 0:
            for qi in range(12):
                quad_free[qi] = []
        else:
            for qi in range(4):
                quad_free[qi] = [regions_free[0]]
            for qi in range(4, 8):
                quad_free[qi] = [regions_free[1]]
            for qi in range(8, 12):
                quad_free[qi] = [sOut_free]
        for tb in range(4):
            lvh = [0, 0]
            sloth = [0, 0]
            for half in range(2):
                th = tb * 2 + half
                slot = th % 2
                sloth[half] = slot
                qn = ("sync", "scalar")[th % 2]
                em.op(qn, (lambda e, b=b, tb=tb, half=half, slot=slot: e.dma_start(
                    out=xnT3[slot][:],
                    in_=xnT_scr[:, :, b * T + tb * 512 + half * 256:
                                b * T + tb * 512 + half * 256 + 256].rearrange(
                                    "ec p t -> p ec t"))),
                    wait=[xnT_slot_free[slot]] + (Fbar if b == 0 and th < 2 else []),
                    inc=f"Lxt{slot}", inc_amt=16)
                lvh[half] = em.now(f"Lxt{slot}")
            for co in range(16):
                for half in range(2):
                    slot = sloth[half]
                    qi = (2 * co + half) % 12
                    for ec in range(8):
                        em.op("tensor", (lambda e, co=co, ec=ec, slot=slot, qi=qi:
                                         e.matmul(
                            quad_ap(qi), wqk_bf[:, ec, co * 128:(co + 1) * 128],
                            xnT3[slot][:, ec, :],
                            start=(ec == 0), stop=(ec == 7))),
                            wait=(([(f"Lxt{slot}", lvh[half])] + quad_free[qi]
                                   + (wqk_waits if b == 0 else []))
                                  if ec == 0 else []),
                            inc="P" if ec == 7 else None)
                    if co == 15:
                        xnT_slot_free[slot] = ("P", em.now("P"))
                P_co = em.now("P")
                qi0 = (2 * co) % 12
                t_ = [big0, big1, sOut][qi0 // 4]
                c0 = (qi0 % 4) * 256
                if co < 8:
                    em.op("scalar", (lambda e, co=co, t_=t_, c0=c0, tb=tb: e.activation(
                        out=QT[:, co, tb * 512:(tb + 1) * 512], in_=t_[:, c0:c0 + 512],
                        func=AF.Identity, bias=biasS[:, co:co + 1], scale=0.125)),
                        wait=[("P", P_co), ("V", V_bias)], inc="A")
                else:
                    cc = co - 8
                    em.op("scalar", (lambda e, cc=cc, co=co, t_=t_, c0=c0, tb=tb:
                                     e.activation(
                        out=KT_bd[0:64, cc, 0, tb * 512:(tb + 1) * 512],
                        in_=t_[0:64, c0:c0 + 512],
                        func=AF.Identity, bias=biasS[0:64, co:co + 1], scale=1.0)),
                        wait=[("P", P_co), ("V", V_bias)]
                        + ([("V", V_kbdz)] if b == 0 else []), inc="A")
                    em.op("scalar", (lambda e, cc=cc, co=co, t_=t_, c0=c0, tb=tb:
                                     e.activation(
                        out=KT_bd[64:128, cc, 1, tb * 512:(tb + 1) * 512],
                        in_=t_[64:128, c0:c0 + 512],
                        func=AF.Identity, bias=biasS[64:128, co:co + 1], scale=1.0)),
                        inc="A")
                qf = [("A", em.now("A"))]
                quad_free[qi0] = qf
                quad_free[(qi0 + 1) % 12] = qf
        A_proj = em.now("A")


        for i in range(NTILE):
            Lk = 128 * (i + 1)
            ib = i % 2
            Lh = Lk // 2
            gw = [gum_free[ib]] + (Fbar if b == 0 and i < 2 else [])
            em.op("sync", (lambda e, b=b, i=i, Lh=Lh, ib=ib: e.dma_start(
                out=gum_b[ib][:, 0:Lh], in_=gum_d[b, i * 128:(i + 1) * 128, 0:Lh])),
                wait=gw, inc=f"Lg{ib}", inc_amt=16)
            em.op("scalar", (lambda e, b=b, i=i, Lh=Lh, Lk=Lk, ib=ib: e.dma_start(
                out=gum_b[ib][:, Lh:Lk], in_=gum_d[b, i * 128:(i + 1) * 128, Lh:Lk])),
                wait=gw, inc=f"Lg{ib}", inc_amt=16)
            L_gum = em.now(f"Lg{ib}")
            if i == 0:
                em.op("sync", (lambda e, b=b: e.dma_start(
                    out=gum0_sb[:], in_=gum0_d[b])),
                    wait=[gum0_free] + (Fbar if b == 0 else []),
                    inc="Lg0f", inc_amt=16)
                L_gum0 = em.now("Lg0f")
            em.op("sync", (lambda e, b=b, i=i: e.dma_start(
                out=xres[:], in_=x_d[b, i * 128:(i + 1) * 128, :])),
                wait=[xres_free], inc="Lr0", inc_amt=16)
            L_xres = em.now("Lr0")
            em.op("sync", (lambda e, b=b, i=i, ib=ib: e.dma_start(
                out=gt_b[ib][:], in_=g_scr[b * T + i * 128:b * T + (i + 1) * 128, :])),
                wait=[gt_free[ib]], inc=f"Lt{ib}", inc_amt=16)
            L_gt = em.now(f"Lt{ib}")

            n_d = (Lk - 128) // 512
            for h in range(NHEAD):
                hp = h % 2
                sl4 = h % 4
                R = region[hp]
                S_ = Sl[sl4]
                p = h // 2
                par = h % 2
                parts = [(0, min(Lk, 1024))] + ([(1024, Lk)] if Lk > 1024 else [])
                for (pi, (c0, c1)) in enumerate(parts):
                    nlo = c0 // 512
                    nhi = (c1 + 511) // 512
                    for n in range(nlo, nhi):
                        k0 = 512 * n
                        w_n = min(512, Lk - k0)
                        rs = k0 - c0
                        is_diag = (n == n_d)
                        last = (n == nhi - 1)
                        wv = []
                        if n == nlo:
                            wv.append(regions_free[hp])
                            if h == 0 and pi == 0:
                                wv.append(("A", A_proj))
                        em.op("tensor", (lambda e, p=p, par=par, i=i, k0=k0, w_n=w_n,
                                         rs=rs, R=R, is_diag=is_diag: e.matmul(
                            R[:, rs:rs + w_n],
                            QT[:, p, i * 128:(i + 1) * 128],
                            KT_bd[:, p, par, k0:k0 + w_n],
                            start=True, stop=not is_diag)),
                            wait=wv,
                            inc="P" if (last and not is_diag) else None)
                        if is_diag:
                            em.op("tensor", (lambda e, R=R, rs=rs, w_n=w_n: e.matmul(
                                R[:, rs + w_n - 128:rs + w_n],
                                ident[:], maskneg[:],
                                start=False, stop=True)),
                                inc="P" if last else None)
                    P_sc = em.now("P")
                    zc = sl4 * 2 + pi
                    ew = [("P", P_sc), ("V", V_bias1)]
                    if pi == 0:
                        ew.append(("V", accV_done[sl4]))
                        if sl4 == 0:
                            ew += E0_free
                        elif sl4 == 1:
                            ew += E1_free
                        if b == 0 and i == 0:
                            ew += Fbar
                    em.op("scalar", (lambda e, R=R, c0=c0, c1=c1, S_=S_, zc=zc:
                                     e.activation(
                        out=S_[:, c0:c1], in_=R[:, 0:c1 - c0], func=AF.Exp,
                        bias=bias1[:], scale=1.0,
                        accum_out=Zp[:, zc:zc + 1])),
                        wait=ew, inc="A")
                    regions_free[hp] = ("A", em.now("A"))
                A_exp = em.now("A")
                cw = [("A", A_exp), ("V", last_stt)]
                if len(parts) == 2:
                    em.op("vector", (lambda e, sl4=sl4: e.tensor_tensor(
                        out=z01[:], in0=Zp[:, 2 * sl4:2 * sl4 + 1],
                        in1=Zp[:, 2 * sl4 + 1:2 * sl4 + 2], op=ALU.add)),
                        wait=cw, inc="V")
                    em.op("vector", lambda e: e.reciprocal(out=cvecV[:], in_=z01[:]),
                          wait=[("V", em.now("V"))], inc="V")
                elif i == 0:
                    em.op("vector", (lambda e, sl4=sl4: e.tensor_scalar(
                        out=z01[:], in0=Zp[:, 2 * sl4:2 * sl4 + 1], scalar1=1e-37,
                        scalar2=None, op0=ALU.max)),
                        wait=cw, inc="V")
                    em.op("vector", lambda e: e.reciprocal(out=cvecV[:], in_=z01[:]),
                          wait=[("V", em.now("V"))], inc="V")
                else:
                    em.op("vector", (lambda e, sl4=sl4: e.reciprocal(
                        out=cvecV[:], in_=Zp[:, 2 * sl4:2 * sl4 + 1])),
                        wait=cw, inc="V")
                V_c = em.now("V")
                if h == 0:
                    em.op("vector", (lambda e, S_=S_, Lk=Lk: e.tensor_scalar(
                        out=avgV[:, 0:Lk], in0=S_[:, 0:Lk], scalar1=cvecV[:],
                        scalar2=None, op0=ALU.mult)),
                        wait=[avgV_free, ("V", V_c)], inc="V")
                else:
                    em.op("vector", (lambda e, S_=S_, Lk=Lk: e.scalar_tensor_tensor(
                        out=avgV[:, 0:Lk], in0=S_[:, 0:Lk], scalar=cvecV[:],
                        in1=avgV[:, 0:Lk], op0=ALU.mult, op1=ALU.add)),
                        wait=[("V", V_c)], inc="V")
                last_stt = em.now("V")
                accV_done[sl4] = last_stt

            if i == 0:
                em.op("vector", (lambda e, Lk=Lk: e.scalar_tensor_tensor(
                    out=S0[:, 0:Lk], in0=avgV[:, 0:Lk], scalar=0.03125,
                    in1=gum0_sb[:], op0=ALU.mult, op1=ALU.add)),
                    wait=[("Lg0f", L_gum0), ("V", last_stt), ("A", A_exp)], inc="V")
                gum0_free = ("V", em.now("V"))
            else:
                em.op("vector", (lambda e, ib=ib, Lk=Lk: e.scalar_tensor_tensor(
                    out=S0[:, 0:Lk], in0=avgV[:, 0:Lk], scalar=0.03125,
                    in1=gum_b[ib][:, 0:Lk], op0=ALU.mult, op1=ALU.add)),
                    wait=[(f"Lg{ib}", L_gum), ("V", last_stt), ("A", A_exp)], inc="V")
            V_l2 = em.now("V")
            avgV_free = ("V", V_l2)
            gum_free[ib] = ("V", V_l2)
            em.op("scalar", (lambda e, ib=ib, Lk=Lk: e.activation(
                out=S1[:, 0:Lk], in_=S0[:, 0:Lk], func=AF.Exp,
                accum_out=Z2[:, ib:ib + 1])),
                wait=[("V", V_l2), lastT_P, z2_free[ib]], inc="A")
            A_w = em.now("A")
            E0_free = [("A", A_w), ("V", V_l2)]
            if i == 0:
                em.op("vector", (lambda e, ib=ib: e.tensor_scalar(
                    out=z01[:], in0=Z2[:, ib:ib + 1], scalar1=1e-37, scalar2=None,
                    op0=ALU.max)),
                    wait=[("A", A_w), ("V", last_comb)], inc="V")
                em.op("vector", lambda e: e.reciprocal(out=rec2[:], in_=z01[:]),
                      wait=[("V", em.now("V"))], inc="V")
            else:
                em.op("vector", (lambda e, ib=ib: e.reciprocal(
                    out=rec2[:], in_=Z2[:, ib:ib + 1])),
                    wait=[("A", A_w), ("V", last_comb)], inc="V")
            V_rec2 = em.now("V")
            z2_free[ib] = ("V", V_rec2)

            for j in range(i + 1):
                sl2 = j % 2
                em.op("tensor", (lambda e, j=j, sl2=sl2: e.matmul(
                    ptA[sl2][:], S1[:, j * 128:(j + 1) * 128], ident[:],
                    start=True, stop=True)),
                    wait=[("A", A_w), pt_free[sl2]], inc="P")
                pv = em.now("P")
                em.op("vector", (lambda e, sl2=sl2: e.tensor_copy(
                    out=wT2[:, sl2, :], in_=ptA[sl2][:])),
                    wait=[("P", pv)], inc="V")
                ev = em.now("V")
                pt_free[sl2] = ("V", ev)
                for n2 in range(2):
                    em.op("tensor", (lambda e, j=j, sl2=sl2, n2=n2, i=i: e.matmul(
                        sOut[:, n2 * 512:(n2 + 1) * 512],
                        wT2[:, sl2, :],
                        xn_a[:, j, n2 * 512:(n2 + 1) * 512],
                        start=(j == 0), stop=(j == i))),
                        wait=([("V", ev), xn_tile_w[j]] +
                              ([sOut_free, ("A", A_proj)] if j == 0 and n2 == 0
                               else [])),
                        inc="P" if (j == i and n2 == 1) else None)
            P_s = em.now("P")
            lastT_P = ("P", P_s)
            E1_free = [("V", last_stt), ("P", P_s)]

            em.op("vector", (lambda e, ib=ib: e.scalar_tensor_tensor(
                out=gt_b[ib][:], in0=sOut[:], scalar=rec2[:], in1=gt_b[ib][:],
                op0=ALU.mult, op1=ALU.mult)),
                wait=[("P", P_s), (f"Lt{ib}", L_gt), ("V", V_rec2)], inc="V")
            if i == 0:
                em.op("vector", (lambda e, ib=ib: e.memset(gt_b[ib][0:1, :], 0.0)),
                      wait=[("V", em.now("V"))], inc="V")
            em.op("vector", (lambda e, ib=ib: e.tensor_tensor(
                out=xres[:], in0=xres[:], in1=gt_b[ib][:], op=ALU.add)),
                wait=[("Lr0", L_xres), ("V", em.now("V"))], inc="V")
            V_comb = em.now("V")
            last_comb = V_comb
            sOut_free = ("V", V_comb)
            gt_free[ib] = ("V", V_comb)
            em.op("gpsimd", (lambda e, b=b, i=i: e.dma_start(
                out=out_d[b, i * 128:(i + 1) * 128, :], in_=xres[:])),
                wait=[("V", V_comb)], inc="So0", inc_amt=16)
            xres_free = ("So0", em.now("So0"))

        xn_a_lastuse = ("P", em.now("P"))

    em.op("gpsimd", None, wait=[("So0", em.now("So0"))])

    sem_guards = []
    semh = {}
    for name in SEMS:
        g = nc.semaphore(f"sem_{name}")
        semh[name] = g.__enter__()
        sem_guards.append(g)
    blk = nc.Block()
    block = blk.__enter__()

    block.sync(lambda e: em.replay("sync", e, semh))
    block.gpsimd(lambda e: em.replay("gpsimd", e, semh))
    block.scalar(lambda e: em.replay("scalar", e, semh))
    block.vector(lambda e: em.replay("vector", e, semh))
    block.tensor(lambda e: em.replay("tensor", e, semh))

    blk.__exit__(None, None, None)
    for g in reversed(sem_guards):
        g.__exit__(None, None, None)
    for g in reversed(ps_guards):
        g.__exit__(None, None, None)
    return nc


_GUM = None
_GUM0 = None
_NC = None


def _gumbel_rows(flat_idx):
    lo = flat_idx.astype(np.uint32)
    k0 = np.uint32(0)
    k1 = np.uint32(42)
    ks = (k0, k1, np.uint32(k0 ^ k1 ^ np.uint32(0x1BD11BDA)))
    rot0 = (13, 15, 26, 6)
    rot1 = (17, 29, 16, 24)
    with np.errstate(over="ignore"):
        x0 = np.full_like(lo, ks[0])
        x1 = lo + ks[1]

        def rounds(x0, x1, rots):
            for r in rots:
                x0 = x0 + x1
                x1 = (x1 << np.uint32(r)) | (x1 >> np.uint32(32 - r))
                x1 = x1 ^ x0
            return x0, x1

        x0, x1 = rounds(x0, x1, rot0); x0 = x0 + ks[1]; x1 = x1 + ks[2] + np.uint32(1)
        x0, x1 = rounds(x0, x1, rot1); x0 = x0 + ks[2]; x1 = x1 + ks[0] + np.uint32(2)
        x0, x1 = rounds(x0, x1, rot0); x0 = x0 + ks[0]; x1 = x1 + ks[1] + np.uint32(3)
        x0, x1 = rounds(x0, x1, rot1); x0 = x0 + ks[1]; x1 = x1 + ks[2] + np.uint32(4)
        x0, x1 = rounds(x0, x1, rot0); x0 = x0 + ks[2]; x1 = x1 + ks[0] + np.uint32(5)
        bits = x0 ^ x1
    f = (bits >> np.uint32(9)) | np.uint32(0x3F800000)
    fl = f.view(np.float32) - np.float32(1.0)
    minval = np.float32(1e-6)
    maxval = np.float32(1.0 - 1e-6)
    u = np.maximum(minval, fl * (maxval - minval) + minval)
    return -np.log(-np.log(u))


def _gumbel():
    global _GUM, _GUM0
    if _GUM is None:
        g = np.empty((32, T, T), np.float16)
        g0 = np.empty((32, 128, 128), np.float32)
        base = np.arange(T * T, dtype=np.uint64)
        idx = np.arange(T)
        maskrow = idx[None, :] >= idx[:, None]
        for b in range(32):
            gb = (np.float32(0.5)
                  * _gumbel_rows(base + np.uint64(b) * np.uint64(T * T))
                  ).reshape(T, T)
            g0[b] = gb[0:128, 0:128]
            g0[b][maskrow[0:128, 0:128]] = np.float32(NEG)
            g[b] = gb.astype(np.float16)
        g[:, maskrow] = np.float16(-20000.0)
        _GUM = g
        _GUM0 = g0
    return _GUM


def kernel(L0_attn_hidden_states, L0_attn_in_proj_weight, L0_attn_in_proj_bias,
           L0_ffn_w_gate, L0_ffn_w_up, L0_ffn_w_down):
    global _NC
    x = np.ascontiguousarray(np.asarray(L0_attn_hidden_states, dtype=np.float32))
    wqkT = np.ascontiguousarray(np.asarray(L0_attn_in_proj_weight,
                                           dtype=np.float32)[:2 * E].T)
    biasqk = np.ascontiguousarray(np.asarray(L0_attn_in_proj_bias,
                                             dtype=np.float32)[:2 * E])
    wgT = np.ascontiguousarray(np.asarray(L0_ffn_w_gate, dtype=np.int32).T)
    wuT = np.ascontiguousarray(np.asarray(L0_ffn_w_up, dtype=np.int32).T)
    wdT = np.ascontiguousarray(np.asarray(L0_ffn_w_down, dtype=np.int32).T)
    gum = _gumbel()
    ident = np.eye(128, dtype=ml_dtypes.bfloat16)
    r = np.arange(128)
    maskneg = np.where(r[None, :] >= r[:, None], np.float32(NEG),
                       np.float32(0.0)).astype(ml_dtypes.bfloat16)
    ones1 = np.ones((1, 128), np.float32)
    if _NC is None:
        _NC = build_program()
    in_maps = []
    for c in range(NCORES):
        in_maps.append({
            "x": x[c * BPC:(c + 1) * BPC],
            "gum": gum[c * BPC:(c + 1) * BPC],
            "gum0": _GUM0[c * BPC:(c + 1) * BPC],
            "wqkT": wqkT, "biasqk": biasqk,
            "wgT": wgT, "wuT": wuT, "wdT": wdT,
            "ident": ident, "maskneg": maskneg, "ones1": ones1,
        })
    res = run_bass_kernel_spmd(_NC, in_maps, core_ids=list(range(NCORES)))
    return np.concatenate([r_["out"] for r_ in res.results], axis=0)
